# version 04
# baseline (speedup 1.0000x reference)
import numpy as np
import ml_dtypes
from contextlib import ExitStack
import concourse.bass as bass
import concourse.mybir as mybir
from concourse.bass_utils import run_bass_kernel_spmd

F32 = mybir.dt.float32
F16 = mybir.dt.float16
BF16 = mybir.dt.bfloat16
I32 = mybir.dt.int32
AF = mybir.ActivationFunctionType
ALU = mybir.AluOpType

B = 16
T = 2048
C = 2048
H = 16
HID = 5632
NCORE = 8
SEQ = 2
CH = 128
NCC = C // CH
NHC = HID // CH
NTC = T // CH
RMS_EPS = 1.1920929e-07
LN_EPS = 1e-5
INV_SQRT_D = float(1.0 / np.sqrt(np.float64(CH)))
NEG = -1e9

_DEBUG = set()


class Ctr:
    def __init__(self):
        self.v = 0

    def inc(self, n=1):
        self.v += n
        return self.v


def build_host_consts():
    inv_freq = (1.0 / 10000.0 ** (np.arange(0, CH, 2, dtype=np.float32) / np.float32(CH))).astype(np.float32)
    t = np.arange(T, dtype=np.float32)
    freqs = inv_freq[:, None] * t[None, :]
    cosv = np.cos(freqs).astype(np.float32)
    sinv = np.sin(freqs).astype(np.float32)
    cos_tab = np.concatenate([cosv, cosv], axis=0).astype(np.float32)
    sin_tab = np.concatenate([-sinv, sinv], axis=0).astype(np.float32)

    kk = np.arange(CH)[:, None]
    qq = np.arange(CH)[None, :]
    trimask = np.where(kk > qq, np.float32(NEG), np.float32(0.0)).astype(ml_dtypes.bfloat16)

    ident_bf = np.eye(CH, dtype=ml_dtypes.bfloat16)
    ident_f32 = np.eye(CH, dtype=np.float32)
    ones_bf = np.ones((CH, 1), dtype=ml_dtypes.bfloat16)
    onesrow_f32 = np.ones((1, CH), dtype=np.float32)

    selrow = np.zeros((H, H * CH), dtype=np.float32)
    for h in range(H):
        selrow[h, h * CH:(h + 1) * CH] = 1.0
    colsel = np.zeros((CH, H * H), dtype=ml_dtypes.bfloat16)
    for h in range(H):
        colsel[:, h * H + h] = 1.0

    return dict(cos_tab=cos_tab, sin_tab=sin_tab, trimask=trimask,
                ident_bf=ident_bf, ident_f32=ident_f32, ones_bf=ones_bf,
                onesrow_f32=onesrow_f32, selrow=selrow, colsel=colsel)


def build_program(upto=99, detect_races=True):
    nc = bass.Bass("TRN2", target_bir_lowering=False, detect_race_conditions=detect_races)
    P = {}

    x_in = nc.dram_tensor("x", [SEQ, T, C], F32, kind="ExternalInput")
    w_qkv = nc.dram_tensor("w_qkv", [3 * C, C], F32, kind="ExternalInput")
    w_o = nc.dram_tensor("w_o", [C, C], F32, kind="ExternalInput")
    wq_gate = nc.dram_tensor("wq_gate", [HID, C], I32, kind="ExternalInput")
    wq_up = nc.dram_tensor("wq_up", [HID, C], I32, kind="ExternalInput")
    wq_down = nc.dram_tensor("wq_down", [C, HID], I32, kind="ExternalInput")
    s_gate = nc.dram_tensor("s_gate", [HID], F32, kind="ExternalInput")
    s_up = nc.dram_tensor("s_up", [HID], F32, kind="ExternalInput")
    s_down = nc.dram_tensor("s_down", [C], F32, kind="ExternalInput")
    cos_t = nc.dram_tensor("cos_tab", [CH, T], F32, kind="ExternalInput")
    sin_t = nc.dram_tensor("sin_tab", [CH, T], F32, kind="ExternalInput")
    trimask_t = nc.dram_tensor("trimask", [CH, CH], BF16, kind="ExternalInput")
    ident_bf_t = nc.dram_tensor("ident_bf", [CH, CH], BF16, kind="ExternalInput")
    ident_f32_t = nc.dram_tensor("ident_f32", [CH, CH], F32, kind="ExternalInput")
    ones_bf_t = nc.dram_tensor("ones_bf", [CH, 1], BF16, kind="ExternalInput")
    onesrow_t = nc.dram_tensor("onesrow_f32", [1, CH], F32, kind="ExternalInput")
    selrow_t = nc.dram_tensor("selrow", [H, H * CH], F32, kind="ExternalInput")
    colsel_t = nc.dram_tensor("colsel", [CH, H * H], BF16, kind="ExternalInput")

    out = nc.dram_tensor("out", [SEQ, T, C], F32, kind="ExternalOutput")

    qkv_bf = nc.dram_tensor("qkv_bf", [3 * C, C], BF16)
    wo_bf = nc.dram_tensor("wo_bf", [C, C], BF16)
    wg_bf = nc.dram_tensor("wg_bf", [HID, C], BF16)
    wu_bf = nc.dram_tensor("wu_bf", [HID, C], BF16)
    wd_bf = nc.dram_tensor("wd_bf", [C, HID], BF16)
    qkvT = nc.dram_tensor("qkvT", [C, 3 * C], BF16)
    woT = nc.dram_tensor("woT", [C, C], BF16)
    wgT = nc.dram_tensor("wgT", [C, HID], BF16)
    wuT = nc.dram_tensor("wuT", [C, HID], BF16)
    wdT = nc.dram_tensor("wdT", [HID, C], BF16)
    xbf = nc.dram_tensor("xbf", [SEQ, T, C], BF16)
    xbfT = nc.dram_tensor("xbfT", [SEQ, C, T], BF16)
    qscr = nc.dram_tensor("qscr", [SEQ, C, T], BF16)
    aoscr = nc.dram_tensor("aoscr", [SEQ, C, T], BF16)
    sscr = nc.dram_tensor("sscr", [SEQ, C, T], BF16)
    s2scr = nc.dram_tensor("s2scr", [SEQ, C, T], BF16)

    dbg = {}
    for name, shape, dt in [
        ("dbg_q", [C, T], F32), ("dbg_k", [C, T], F32), ("dbg_v", [T, C], F32),
        ("dbg_ao", [C, T], F32), ("dbg_s", [C, T], F32), ("dbg_rq", [H, T], F32),
        ("dbg_rk", [H, T], F32), ("dbg_rkt", [CH, NTC * H], F32),
        ("dbg_x1", [C, T], F32), ("dbg_m", [HID, T], F32),
        ("dbg_mb", [CH, T], F32), ("dbg_rbt", [CH, T], F32),
        ("dbg_s2", [C, T], F32),
    ]:
        if name in _DEBUG:
            dbg[name] = nc.dram_tensor(name, shape, dt, kind="ExternalOutput")

    ctx = ExitStack()
    sS = ctx.enter_context(nc.semaphore("sS"))
    sG = ctx.enter_context(nc.semaphore("sG"))
    sP = ctx.enter_context(nc.semaphore("sP"))
    sA = ctx.enter_context(nc.semaphore("sA"))
    sV = ctx.enter_context(nc.semaphore("sV"))
    sL = ctx.enter_context(nc.semaphore("sL"))
    cS, cG, cP, cA, cV, cL = (Ctr() for _ in range(6))

    g_ident_bf = ctx.enter_context(nc.sbuf_tensor("g_ident_bf", [CH, CH], BF16))
    g_ident_f32 = ctx.enter_context(nc.sbuf_tensor("g_ident_f32", [CH, CH], F32))
    g_trimask = ctx.enter_context(nc.sbuf_tensor("g_trimask", [CH, CH], BF16))
    g_ones_bf = ctx.enter_context(nc.sbuf_tensor("g_ones_bf", [CH, 1], BF16))
    g_onesrow = ctx.enter_context(nc.sbuf_tensor("g_onesrow", [1, CH], F32))
    g_colsel = ctx.enter_context(nc.sbuf_tensor("g_colsel", [CH, H * H], BF16))
    g_sg = ctx.enter_context(nc.sbuf_tensor("g_sg", [CH, NHC], F32))
    g_su = ctx.enter_context(nc.sbuf_tensor("g_su", [CH, NHC], F32))
    g_sd = ctx.enter_context(nc.sbuf_tensor("g_sd", [CH, NCC], F32))
    g_sg_b = ctx.enter_context(nc.sbuf_tensor("g_sg_b", [CH, NHC], F32))
    g_su_b = ctx.enter_context(nc.sbuf_tensor("g_su_b", [CH, NHC], F32))
    g_sd_b = ctx.enter_context(nc.sbuf_tensor("g_sd_b", [CH, NCC], F32))
    g_mb = ctx.enter_context(nc.sbuf_tensor("g_mb", [CH, T], BF16))
    g_rb = ctx.enter_context(nc.sbuf_tensor("g_rb", [CH, T], F32))
    g_rstd_q = ctx.enter_context(nc.sbuf_tensor("g_rstd_q", [H, T], F32))
    g_rkT = ctx.enter_context(nc.sbuf_tensor("g_rkT", [CH, NTC * H], F32))

    P.update(locals())

    with nc.Block() as blk:
        @blk.gpsimd
        def _(g):
            for lo in (0, C):
                src = w_qkv.ap()[lo:lo + C, :].rearrange("(h d two) c -> two h d c", h=H, d=64, two=2)
                dst = qkv_bf.ap()[lo:lo + C, :].rearrange("(h two d) c -> two h d c", h=H, two=2, d=64)
                for half in (0, 1):
                    g.dma_start(dst[half], src[half]).then_inc(sG, 16); cG.inc()
            g.dma_start(qkv_bf.ap()[2 * C:3 * C, :], w_qkv.ap()[2 * C:3 * C, :]).then_inc(sG, 16); cG.inc()
            g.dma_start(wo_bf.ap(), w_o.ap()).then_inc(sG, 16); cG.inc()
            for si in range(SEQ):
                g.dma_start(xbf.ap()[si], x_in.ap()[si]).then_inc(sG, 16); cG.inc()
        @blk.sync
        def _(s):
            for dst, src in [(g_ident_bf, ident_bf_t), (g_ident_f32, ident_f32_t),
                             (g_trimask, trimask_t), (g_ones_bf, ones_bf_t),
                             (g_onesrow, onesrow_t), (g_colsel, colsel_t)]:
                s.dma_start(dst[:, :], src.ap()).then_inc(sS, 16); cS.inc()
            s.wait_ge(sS, 16 * cS.v)
            s.wait_ge(sG, 16 * cG.v)

    if "skip_p1" in _DEBUG:
        _skip_p1 = True
    with ExitStack() as _st:
        p1_srows = _st.enter_context(nc.sbuf_tensor("p1_srows", [NHC, CH], F32))
        p1_ps = _st.enter_context(nc.psum_tensor("p1_ps", [CH, NHC], F32))
        for (svec, n, gdst, gbias) in [] if "skip_p1" in _DEBUG else [(s_gate, NHC, g_sg, g_sg_b),
                                       (s_up, NHC, g_su, g_su_b),
                                       (s_down, NCC, g_sd, g_sd_b)]:
            with nc.Block() as blk:
                @blk.sync
                def _(s):
                    s.dma_start(p1_srows[0:n, :], svec.ap().rearrange("(n p) -> n p", p=CH)).then_inc(sS, 16); cS.inc()
                @blk.tensor
                def _(t):
                    t.wait_ge(sS, 16 * cS.v)
                    t.transpose(p1_ps[:, 0:n], p1_srows[0:n, :], g_ident_f32[0:n, 0:n]).then_inc(sP, 1); cP.inc()
                @blk.vector
                def _(v):
                    v.wait_ge(sP, cP.v)
                    v.tensor_copy(gdst[:, 0:n], p1_ps[:, 0:n]).then_inc(sV, 1); cV.inc()
                    v.wait_ge(sV, cV.v)
                    v.tensor_scalar_mul(gbias[:, 0:n], gdst[:, 0:n], -128.0)

    with ExitStack() as _st:
        p1_in0 = _st.enter_context(nc.sbuf_tensor("p1_in0", [CH, HID], I32))
        p1_in1 = _st.enter_context(nc.sbuf_tensor("p1_in1", [CH, HID], I32))
        p1_out0 = _st.enter_context(nc.sbuf_tensor("p1_out0", [CH, HID], BF16))
        p1_out1 = _st.enter_context(nc.sbuf_tensor("p1_out1", [CH, HID], BF16))
        jobs = [] if "skip_p1" in _DEBUG else ([(wq_gate, i, C, g_sg, g_sg_b, wg_bf) for i in range(NHC)]
                + [(wq_up, i, C, g_su, g_su_b, wu_bf) for i in range(NHC)]
                + [(wq_down, i, HID, g_sd, g_sd_b, wd_bf) for i in range(NCC)])
        with nc.Block() as blk:
            bS, bV, bG = cS.v, cV.v, cG.v
            @blk.sync
            def _(s):
                for j, (src, i, L, _sc, _bi, _dst) in enumerate(jobs):
                    buf = (p1_in0, p1_in1)[j % 2]
                    if j >= 2:
                        s.wait_ge(sV, bV + j - 1)
                    s.dma_start(buf[:, 0:L], src.ap()[i * CH:(i + 1) * CH, :]).then_inc(sS, 16); cS.inc()
            @blk.vector
            def _(v):
                for j, (src, i, L, sc, bi, _dst) in enumerate(jobs):
                    buf = (p1_in0, p1_in1)[j % 2]
                    obuf = (p1_out0, p1_out1)[j % 2]
                    v.wait_ge(sS, 16 * (bS + j + 1))
                    if j >= 2:
                        v.wait_ge(sG, 16 * (bG + j - 1))
                    v.tensor_scalar(obuf[:, 0:L], buf[:, 0:L], sc[:, i:i + 1], bi[:, i:i + 1],
                                    ALU.mult, ALU.add).then_inc(sV, 1); cV.inc()
            @blk.gpsimd
            def _(g):
                for j, (_src, i, L, _sc, _bi, dst) in enumerate(jobs):
                    obuf = (p1_out0, p1_out1)[j % 2]
                    g.wait_ge(sV, bV + j + 1)
                    g.dma_start(dst.ap()[i * CH:(i + 1) * CH, :], obuf[:, 0:L]).then_inc(sG, 16); cG.inc()
                g.wait_ge(sG, 16 * cG.v)

    with ExitStack() as _st:
        p2_buf0 = _st.enter_context(nc.sbuf_tensor("p2_buf0", [CH, 3 * C], BF16))
        p2_buf1 = _st.enter_context(nc.sbuf_tensor("p2_buf1", [CH, 3 * C], BF16))
        trips = ([] if "skip_p2full" in _DEBUG else [(qkv_bf, i, 3 * C, qkvT) for i in range(NCC)]
                 + ([] if ("skip_p2ffn" in _DEBUG or "skip_p2full" in _DEBUG) else
                    [(wo_bf, i, C, woT) for i in range(NCC)]
                    + [(wg_bf, i, HID, wgT) for i in range(NCC)]
                    + [(wu_bf, i, HID, wuT) for i in range(NCC)]
                    + [(wd_bf, i, C, wdT) for i in range(NHC)]))
        with nc.Block() as blk:
            bS, bG = cS.v, cG.v
            @blk.sync
            def _(s):
                for j, (src, i, L, _dst) in enumerate(trips):
                    buf = (p2_buf0, p2_buf1)[j % 2]
                    if j >= 2:
                        s.wait_ge(sG, 16 * (bG + j - 1))
                    s.dma_start_transpose(buf[:, 0:L], src.ap()[:, i * CH:(i + 1) * CH]).then_inc(sS, 16); cS.inc()
            @blk.gpsimd
            def _(g):
                for j, (_src, i, L, dst) in enumerate(trips):
                    buf = (p2_buf0, p2_buf1)[j % 2]
                    g.wait_ge(sS, 16 * (bS + j + 1))
                    g.dma_start(dst.ap()[i * CH:(i + 1) * CH, :], buf[:, 0:L]).then_inc(sG, 16); cG.inc()
                g.wait_ge(sG, 16 * cG.v)

    if upto >= 1:
        for s_i in range(SEQ):
            _emit_seq(nc, P, s_i, ctx, dbg,
                      (sS, sG, sP, sA, sV, sL), (cS, cG, cP, cA, cV, cL), upto)
            if "single_seq" in _DEBUG:
                break

    with nc.Block() as blk:
        @blk.sync
        def _(s):
            s.wait_ge(sS, 16 * cS.v)
            s.wait_ge(sG, 16 * cG.v)

    ctx.close()
    return nc, dbg


def _emit_seq(nc, P, s_i, ctx, dbg, sems, ctrs, upto=99):
    sS, sG, sP, sA, sV, sL = sems
    cS, cG, cP, cA, cV, cL = ctrs
    xbf, xbfT, qscr, aoscr, sscr = P["xbf"], P["xbfT"], P["qscr"], P["aoscr"], P["sscr"]
    s2scr = P["s2scr"]
    qkvT, woT, wgT, wuT, wdT = P["qkvT"], P["woT"], P["wgT"], P["wuT"], P["wdT"]
    cos_t, sin_t, out = P["cos_t"], P["sin_t"], P["out"]
    g_ident_bf, g_ident_f32 = P["g_ident_bf"], P["g_ident_f32"]
    g_trimask, g_ones_bf, g_onesrow = P["g_trimask"], P["g_ones_bf"], P["g_onesrow"]
    g_colsel, g_mb, g_rb = P["g_colsel"], P["g_mb"], P["g_rb"]
    g_rstd_q, g_rkT = P["g_rstd_q"], P["g_rkT"]
    selrow_t = P["selrow_t"]

    if "skip_apre" in _DEBUG:
        pass
    with ExitStack() as _st:
        ap_b0 = _st.enter_context(nc.sbuf_tensor(f"ap{s_i}_b0", [CH, T], BF16))
        ap_b1 = _st.enter_context(nc.sbuf_tensor(f"ap{s_i}_b1", [CH, T], BF16))
        _nap = NCC if "skip_apre" not in _DEBUG else 0
        with nc.Block() as blk:
            bS, bG = cS.v, cG.v
            @blk.sync
            def _(s):
                for i in range(_nap):
                    buf = (ap_b0, ap_b1)[i % 2]
                    if i >= 2:
                        s.wait_ge(sG, 16 * (bG + i - 1))
                    s.dma_start_transpose(buf[:, :], xbf.ap()[s_i][:, i * CH:(i + 1) * CH]).then_inc(sS, 16); cS.inc()
            @blk.gpsimd
            def _(g):
                for i in range(_nap):
                    buf = (ap_b0, ap_b1)[i % 2]
                    g.wait_ge(sS, 16 * (bS + i + 1))
                    g.dma_start(xbfT.ap()[s_i][i * CH:(i + 1) * CH, :], buf[:, :]).then_inc(sG, 16); cG.inc()
                g.wait_ge(sG, 16 * cG.v)

    if upto < 2:
        return
    with ExitStack() as _st:
        k_all = _st.enter_context(nc.sbuf_tensor(f"k_all{s_i}", [CH, NCC * T], BF16))
        with ExitStack() as _st:
            akq_x = _st.enter_context(nc.sbuf_tensor(f"akq{s_i}_x", [CH, NCC * 1024], BF16))
            akq_w0 = _st.enter_context(nc.sbuf_tensor(f"akq{s_i}_w0", [CH, NCC * CH], BF16))
            akq_w1 = _st.enter_context(nc.sbuf_tensor(f"akq{s_i}_w1", [CH, NCC * CH], BF16))
            akq_cos = _st.enter_context(nc.sbuf_tensor(f"akq{s_i}_cos", [CH, 1024], F32))
            akq_sin = _st.enter_context(nc.sbuf_tensor(f"akq{s_i}_sin", [CH, 1024], F32))
            akq_sq = _st.enter_context(nc.sbuf_tensor(f"akq{s_i}_sq", [CH, 1024], BF16))
            akq_t1 = _st.enter_context(nc.sbuf_tensor(f"akq{s_i}_t1", [CH, 1024], BF16))
            akq_rot = _st.enter_context(nc.sbuf_tensor(f"akq{s_i}_rot", [CH, 1024], BF16))
            akq_t2 = _st.enter_context(nc.sbuf_tensor(f"akq{s_i}_t2", [CH, 1024], BF16))
            akq_q0 = _st.enter_context(nc.sbuf_tensor(f"akq{s_i}_q0", [CH, 1024], BF16))
            akq_q1 = _st.enter_context(nc.sbuf_tensor(f"akq{s_i}_q1", [CH, 1024], BF16))
            akq_ssqr = _st.enter_context(nc.sbuf_tensor(f"akq{s_i}_ssqr", [H, T], F32))
            rk_rows = _st.enter_context(nc.sbuf_tensor(f"rkrows{s_i}", [H, T], F32))
            akq_ps0 = _st.enter_context(nc.psum_tensor(f"akq{s_i}_ps0", [CH, 1024], F32))
            akq_ps1 = _st.enter_context(nc.psum_tensor(f"akq{s_i}_ps1", [CH, 1024], F32))
            akq_ssq = _st.enter_context(nc.psum_tensor(f"akq{s_i}_ssq", [H, T], F32))
            sects = ("k",) if "akq_k0" in _DEBUG else ("k", "q")
            for sect in sects:
                o_base = C if sect == "k" else 0
                for th in ((0,) if "akq_k0" in _DEBUG else (0, 1)):
                    tc = slice(th * 1024, (th + 1) * 1024)
                    with nc.Block() as blk:
                        bS, bG, bP, bA, bV, bL = cS.v, cG.v, cP.v, cA.v, cV.v, cL.v
                        @blk.sync
                        def _(s):
                            s.dma_start(akq_cos[:, :], cos_t.ap()[:, tc]).then_inc(sS, 16); cS.inc()
                            s.dma_start(akq_sin[:, :], sin_t.ap()[:, tc]).then_inc(sS, 16); cS.inc()
                            srcx = xbfT.ap()[s_i][:, tc].rearrange("(cc p) t -> p cc t", p=CH)
                            dstx = akq_x[:, :].rearrange("p (cc t) -> p cc t", cc=NCC)
                            s.dma_start(dstx, srcx).then_inc(sS, 16); cS.inc()
                            for oi in range(H):
                                buf = (akq_w0, akq_w1)[oi % 2]
                                if oi >= 2:
                                    s.wait_ge(sP, bP + 2 * oi - 3)
                                srcw = qkvT.ap()[:, o_base + oi * CH: o_base + (oi + 1) * CH]
                                srcw = srcw.rearrange("(cc p) o -> p cc o", p=CH)
                                dstw = buf[:, :].rearrange("p (cc o) -> p cc o", cc=NCC)
                                s.dma_start(dstw, srcw).then_inc(sS, 16); cS.inc()
                        @blk.tensor
                        def _(t):
                            for oi in range(H):
                                buf = (akq_w0, akq_w1)[oi % 2]
                                ps = (akq_ps0, akq_ps1)[oi % 2]
                                t.wait_ge(sS, 16 * (bS + 4 + oi))
                                if oi >= 2:
                                    t.wait_ge(sA, bA + oi - 1)
                                    t.wait_ge(sV, bV + 8 * (oi - 2) + 6)
                                if "stub1" in _DEBUG:
                                    t.nop().then_inc(sP, 1); cP.inc()
                                    t.nop().then_inc(sP, 1); cP.inc()
                                    continue
                                for ci in range(NCC):
                                    for ns in range(2):
                                        ins = t.matmul(ps[:, ns * 512:(ns + 1) * 512],
                                                       buf[:, ci * CH:(ci + 1) * CH],
                                                       akq_x[:, ci * 1024 + ns * 512: ci * 1024 + (ns + 1) * 512],
                                                       start=(ci == 0), stop=(ci == NCC - 1))
                                ins.then_inc(sP, 1); cP.inc()
                                if "no_ssq" in _DEBUG:
                                    t.nop().then_inc(sP, 1); cP.inc()
                                    continue
                                t.wait_ge(sA, bA + oi + 1)
                                for ns in range(2):
                                    ins = t.matmul(akq_ssq[:, th * 1024 + ns * 512: th * 1024 + (ns + 1) * 512],
                                                   g_colsel[:, oi * H:(oi + 1) * H],
                                                   akq_sq[:, ns * 512:(ns + 1) * 512],
                                                   start=(oi == 0), stop=(oi == H - 1))
                                ins.then_inc(sP, 1); cP.inc()
                        @blk.scalar
                        def _(sc):
                            for oi in range(H):
                                ps = (akq_ps0, akq_ps1)[oi % 2]
                                sc.wait_ge(sP, bP + 2 * oi + 1)
                                if "no_ssq" in _DEBUG or "stub1" in _DEBUG or "stub2" in _DEBUG:
                                    sc.nop().then_inc(sA, 1); cA.inc()
                                    continue
                                sc.activation(akq_sq[:, :], ps[:, :], AF.Square).then_inc(sA, 1); cA.inc()
                        @blk.vector
                        def _(v):
                            for oi in range(H):
                                ps = (akq_ps0, akq_ps1)[oi % 2]
                                v.wait_ge(sP, bP + 2 * oi + 1)
                                v.wait_ge(sA, bA + oi + 1)
                                if sect == "q" and oi >= 2:
                                    v.wait_ge(sG, 16 * (bG + oi - 1))
                                if "stub1" in _DEBUG or "stub2" in _DEBUG or "stub3" in _DEBUG:
                                    for _ in range(8):
                                        v.tensor_copy(akq_t1[:, 0:8], akq_cos[:, 0:8]).then_inc(sV, 1); cV.inc()
                                    continue
                                st4 = "stub4" in _DEBUG
                                st5 = "stub5" in _DEBUG
                                st6 = "stub6" in _DEBUG
                                for nn in range(2):
                                    c5 = slice(nn * 512, (nn + 1) * 512)
                                    src0 = akq_sin[:, c5] if "stub7" in _DEBUG else ps[:, c5]
                                    v.tensor_tensor(akq_t1[:, c5], src0, akq_cos[:, c5], ALU.mult).then_inc(sV, 1); cV.inc()
                                    if st4:
                                        v.tensor_copy(akq_rot[0:64, c5], akq_t1[0:64, c5]).then_inc(sV, 1); cV.inc()
                                        v.tensor_copy(akq_rot[64:128, c5], akq_t1[64:128, c5]).then_inc(sV, 1); cV.inc()
                                    else:
                                        v.tensor_copy(akq_rot[0:64, c5], ps[64:128, c5]).then_inc(sV, 1); cV.inc()
                                        v.tensor_copy(akq_rot[64:128, c5], ps[0:64, c5]).then_inc(sV, 1); cV.inc()
                                if st4 or st5:
                                    v.tensor_copy(akq_t2[:, 0:8], akq_rot[:, 0:8]).then_inc(sV, 1); cV.inc()
                                else:
                                    v.tensor_tensor(akq_t2[:, :], akq_rot[:, :], akq_sin[:, :], ALU.mult).then_inc(sV, 1); cV.inc()
                                if st4 or st5 or st6:
                                    v.tensor_copy(akq_t1[:, 0:8], akq_cos[:, 0:8]).then_inc(sV, 1); cV.inc()
                                    continue
                                if sect == "k":
                                    dst = k_all[:, oi * T + th * 1024: oi * T + (th + 1) * 1024]
                                else:
                                    dst = ((akq_q0, akq_q1)[oi % 2])[:, :]
                                v.tensor_tensor(dst, akq_t1[:, :], akq_t2[:, :], ALU.add).then_inc(sV, 1); cV.inc()
                        @blk.gpsimd
                        def _(g):
                            if sect == "q":
                                for oi in range(H):
                                    qbuf = (akq_q0, akq_q1)[oi % 2]
                                    g.wait_ge(sV, bV + 8 * oi + 8)
                                    g.dma_start(qscr.ap()[s_i][oi * CH:(oi + 1) * CH, tc], qbuf[:, :]).then_inc(sG, 16); cG.inc()
                                g.wait_ge(sG, 16 * cG.v)

                if "no_rstd" in _DEBUG:
                    continue
                with nc.Block() as blk:
                    rows = rk_rows if sect == "k" else g_rstd_q
                    bP, bA, bV = cP.v, cA.v, cV.v
                    @blk.vector
                    def _(v):
                        for nn in range(4):
                            c5 = slice(nn * 512, (nn + 1) * 512)
                            ins = v.tensor_scalar(akq_ssqr[:, c5], akq_ssq[:, c5], 1.0 / CH, RMS_EPS,
                                                  ALU.mult, ALU.add)
                        ins.then_inc(sV, 1); cV.inc()
                        v.wait_ge(sA, bA + 1)
                        v.reciprocal(rows[:, :], akq_ssqr[:, :]).then_inc(sV, 1); cV.inc()
                        if sect == "k":
                            v.tensor_scalar_mul(rows[:, :], rows[:, :], INV_SQRT_D).then_inc(sV, 1); cV.inc()
                            for b in range(NTC):
                                v.wait_ge(sP, bP + b + 1)
                                v.tensor_copy(g_rkT[:, b * H:(b + 1) * H],
                                              ((akq_ps0, akq_ps1)[b % 2])[:, 0:H]).then_inc(sV, 1); cV.inc()
                    @blk.scalar
                    def _(sc):
                        sc.wait_ge(sV, bV + 1)
                        sc.activation(akq_ssqr[:, :], akq_ssqr[:, :], AF.Sqrt).then_inc(sA, 1); cA.inc()
                    @blk.tensor
                    def _(t):
                        if sect == "k":
                            t.wait_ge(sV, bV + 3)
                            for b in range(NTC):
                                ps = (akq_ps0, akq_ps1)[b % 2]
                                if b >= 2:
                                    t.wait_ge(sV, bV + 3 + (b - 1))
                                t.transpose(ps[:, 0:H], rows[:, b * CH:(b + 1) * CH],
                                            g_ident_f32[0:H, 0:H]).then_inc(sP, 1); cP.inc()

        if dbg and ("dbg_k" in dbg or "dbg_rq" in dbg or "dbg_rkt" in dbg) and s_i == 0:
            with nc.Block() as blk:
                @blk.gpsimd
                def _(g):
                    if "dbg_k" in dbg:
                        src = k_all[:, :].rearrange("p (cc t) -> p cc t", cc=NCC)
                        dst = dbg["dbg_k"].ap().rearrange("(cc p) t -> p cc t", p=CH)
                        g.dma_start(dst, src).then_inc(sG, 16); cG.inc()
                    if "dbg_rq" in dbg:
                        g.dma_start(dbg["dbg_rq"].ap(), g_rstd_q[:, :]).then_inc(sG, 16); cG.inc()
                    if "dbg_rkt" in dbg:
                        g.dma_start(dbg["dbg_rkt"].ap(), g_rkT[:, :]).then_inc(sG, 16); cG.inc()
                    g.wait_ge(sG, 16 * cG.v)
        if upto < 3:
            _emit_dram_dbg(nc, dbg, s_i, qscr, aoscr, sscr, sG, cG)
            return
        with ExitStack() as _st:
            v_all = _st.enter_context(nc.sbuf_tensor(f"v_all{s_i}", [CH, NTC * C], BF16))
            with ExitStack() as _st:
                av_x0 = _st.enter_context(nc.sbuf_tensor(f"av{s_i}_x0", [CH, NCC * CH], BF16))
                av_x1 = _st.enter_context(nc.sbuf_tensor(f"av{s_i}_x1", [CH, NCC * CH], BF16))
                av_w = _st.enter_context(nc.sbuf_tensor(f"av{s_i}_w", [CH, NCC * 1024], BF16))
                av_ps0 = _st.enter_context(nc.psum_tensor(f"av{s_i}_ps0", [CH, 1024], F32))
                av_ps1 = _st.enter_context(nc.psum_tensor(f"av{s_i}_ps1", [CH, 1024], F32))
                for dh in range(2):
                    with nc.Block() as blk:
                        bS, bP, bA = cS.v, cP.v, cA.v
                        @blk.sync
                        def _(s):
                            srcw = qkvT.ap()[:, 2 * C + dh * 1024: 2 * C + (dh + 1) * 1024]
                            srcw = srcw.rearrange("(cc p) d -> p cc d", p=CH)
                            dstw = av_w[:, :].rearrange("p (cc d) -> p cc d", cc=NCC)
                            s.dma_start(dstw, srcw).then_inc(sS, 16); cS.inc()
                            for ti in range(NTC):
                                buf = (av_x0, av_x1)[ti % 2]
                                if ti >= 2:
                                    s.wait_ge(sP, bP + ti - 1)
                                srcx = xbfT.ap()[s_i][:, ti * CH:(ti + 1) * CH].rearrange("(cc p) t -> p cc t", p=CH)
                                dstx = buf[:, :].rearrange("p (cc t) -> p cc t", cc=NCC)
                                s.dma_start(dstx, srcx).then_inc(sS, 16); cS.inc()
                        @blk.tensor
                        def _(t):
                            for ti in range(NTC):
                                buf = (av_x0, av_x1)[ti % 2]
                                ps = (av_ps0, av_ps1)[ti % 2]
                                t.wait_ge(sS, 16 * (bS + 2 + ti))
                                if ti >= 2:
                                    t.wait_ge(sA, bA + ti - 1)
                                for ci in range(NCC):
                                    for ds in range(2):
                                        ins = t.matmul(ps[:, ds * 512:(ds + 1) * 512],
                                                       buf[:, ci * CH:(ci + 1) * CH],
                                                       av_w[:, ci * 1024 + ds * 512: ci * 1024 + (ds + 1) * 512],
                                                       start=(ci == 0), stop=(ci == NCC - 1))
                                ins.then_inc(sP, 1); cP.inc()
                        @blk.scalar
                        def _(sc):
                            for ti in range(NTC):
                                ps = (av_ps0, av_ps1)[ti % 2]
                                sc.wait_ge(sP, bP + ti + 1)
                                sc.activation(v_all[:, ti * C + dh * 1024: ti * C + (dh + 1) * 1024],
                                              ps[:, :], AF.Copy).then_inc(sA, 1); cA.inc()

            if dbg and "dbg_v" in dbg and s_i == 0:
                with nc.Block() as blk:
                    @blk.gpsimd
                    def _(g):
                        src = v_all[:, :].rearrange("p (ti c) -> p ti c", ti=NTC)
                        dst = dbg["dbg_v"].ap().rearrange("(ti p) c -> p ti c", p=CH)
                        g.dma_start(dst, src).then_inc(sG, 16); cG.inc()
                        g.wait_ge(sG, 16 * cG.v)
            if upto < 4:
                _emit_dram_dbg(nc, dbg, s_i, qscr, aoscr, sscr, sG, cG)
                return
            with ExitStack() as _st:
                b_qg = _st.enter_context(nc.sbuf_tensor(f"b{s_i}_qg", [CH, H * 512], BF16))
                b_qs0 = _st.enter_context(nc.sbuf_tensor(f"b{s_i}_qs0", [CH, 512], BF16))
                b_qs1 = _st.enter_context(nc.sbuf_tensor(f"b{s_i}_qs1", [CH, 512], BF16))
                b_p0 = _st.enter_context(nc.sbuf_tensor(f"b{s_i}_p0", [CH, 512], BF16))
                b_p1 = _st.enter_context(nc.sbuf_tensor(f"b{s_i}_p1", [CH, 512], BF16))
                b_rs = _st.enter_context(nc.sbuf_tensor(f"b{s_i}_rs", [CH, 512], F32))
                b_rcp = _st.enter_context(nc.sbuf_tensor(f"b{s_i}_rcp", [1, 512], F32))
                b_ao = _st.enter_context(nc.sbuf_tensor(f"b{s_i}_ao", [CH, H * 512], BF16))
                b_sel = _st.enter_context(nc.sbuf_tensor(f"b{s_i}_sel", [H, H * CH], F32))
                b_rbps = _st.enter_context(nc.psum_tensor(f"b{s_i}_rb", [CH, 512], F32))
                b_sc0 = _st.enter_context(nc.psum_tensor(f"b{s_i}_sc0", [CH, 512], F32))
                b_sc1 = _st.enter_context(nc.psum_tensor(f"b{s_i}_sc1", [CH, 512], F32))
                b_sm0 = _st.enter_context(nc.psum_tensor(f"b{s_i}_sm0", [1, 512], F32))
                b_sm1 = _st.enter_context(nc.psum_tensor(f"b{s_i}_sm1", [1, 512], F32))
                b_o0 = _st.enter_context(nc.psum_tensor(f"b{s_i}_o0", [CH, 512], F32))
                b_o1 = _st.enter_context(nc.psum_tensor(f"b{s_i}_o1", [CH, 512], F32))
                b_rsps = _st.enter_context(nc.psum_tensor(f"b{s_i}_rsps", [CH, 512], F32))
                with nc.Block() as blk:
                    @blk.sync
                    def _(s):
                        s.dma_start(b_sel[:, :], selrow_t.ap()).then_inc(sS, 16); cS.inc()

                for g_i in range(4):
                    nb = 4 * (g_i + 1)
                    qcol = slice(g_i * 512, (g_i + 1) * 512)
                    with nc.Block() as blk:
                        bS, bG, bP, bA, bV = cS.v, cG.v, cP.v, cA.v, cV.v
                        npe_h = 2 * nb + 2
                        nact_h = nb + 1
                        @blk.sync
                        def _(s):
                            srcq = qscr.ap()[s_i][:, qcol].rearrange("(hc p) t -> p hc t", p=CH)
                            dstq = b_qg[:, :].rearrange("p (hc t) -> p hc t", hc=H)
                            s.dma_start(dstq, srcq).then_inc(sS, 16); cS.inc()
                        @blk.tensor
                        def _(t):
                            for h in range(H):
                                hP = bP + h * npe_h
                                hA = bA + h * nact_h
                                if h == 0:
                                    t.wait_ge(sS, 16 * (bS + 1))
                                if h >= 1:
                                    t.wait_ge(sV, bV + 3 * (h - 1) + 1)
                                t.matmul(b_rbps[:, :], b_sel[:, h * CH:(h + 1) * CH],
                                         g_rstd_q[:, qcol], start=True, stop=True).then_inc(sP, 1); cP.inc()
                                t.wait_ge(sV, bV + 3 * h + 1)
                                def scores(b):
                                    i = b - 4 * g_i
                                    off = 128 * i if i >= 0 else 0
                                    ps = (b_sc0, b_sc1)[b % 2]
                                    if b >= 2:
                                        t.wait_ge(sA, hA + b - 1)
                                    ins = t.matmul(ps[:, off:512],
                                                   k_all[:, h * T + b * CH: h * T + (b + 1) * CH],
                                                   ((b_qs0, b_qs1)[h % 2])[:, off:512],
                                                   start=True, stop=(i < 0))
                                    if i >= 0:
                                        ins = t.matmul(ps[:, off:off + 128], g_ident_bf[:, :],
                                                       g_trimask[:, :], start=False, stop=True)
                                    ins.then_inc(sP, 1); cP.inc()
                                def sums_av(b):
                                    i = b - 4 * g_i
                                    off = 128 * i if i >= 0 else 0
                                    pb = (b_p0, b_p1)[b % 2]
                                    t.wait_ge(sA, hA + b + 1)
                                    t.matmul(((b_sm0, b_sm1)[h % 2])[:, off:512], g_ones_bf[:, :],
                                             pb[:, off:512], start=(b == 0), stop=(b == nb - 1))
                                    ins = t.matmul(((b_o0, b_o1)[h % 2])[:, off:512],
                                                   v_all[:, b * C + h * CH: b * C + (h + 1) * CH],
                                                   pb[:, off:512], start=(b == 0), stop=(b == nb - 1))
                                    ins.then_inc(sP, 1); cP.inc()
                                scores(0)
                                for b in range(1, nb):
                                    scores(b)
                                    sums_av(b - 1)
                                sums_av(nb - 1)
                                t.wait_ge(sV, bV + 3 * h + 2)
                                if h >= 1:
                                    t.wait_ge(sA, bA + h * nact_h)
                                t.matmul(b_rsps[:, :], g_onesrow[:, :], b_rcp[:, :],
                                         start=True, stop=True).then_inc(sP, 1); cP.inc()
                        @blk.scalar
                        def _(sc):
                            for h in range(H):
                                hP = bP + h * npe_h
                                for b in range(nb):
                                    i = b - 4 * g_i
                                    off = 128 * i if i >= 0 else 0
                                    ps = (b_sc0, b_sc1)[b % 2]
                                    pb = (b_p0, b_p1)[b % 2]
                                    spos = 2 if b == 0 else 2 * b + 1
                                    sc.wait_ge(sP, hP + spos)
                                    if b >= 2:
                                        apos = 2 * (b - 2) + 4 if (b - 2) < nb - 1 else 2 * nb + 1
                                        sc.wait_ge(sP, hP + apos)
                                    sc.activation(pb[:, off:512], ps[:, off:512], AF.Exp,
                                                  scale=g_rkT[:, b * H + h: b * H + h + 1]).then_inc(sA, 1); cA.inc()
                                sc.wait_ge(sP, hP + 2 * nb + 2)
                                if h >= 1:
                                    sc.wait_ge(sV, bV + 3 * (h - 1) + 3)
                                sc.activation(b_rs[:, :], b_rsps[:, :], AF.Copy).then_inc(sA, 1); cA.inc()
                        @blk.vector
                        def _(v):
                            for h in range(H):
                                hP = bP + h * npe_h
                                hA = bA + h * nact_h
                                v.wait_ge(sP, hP + 1)
                                v.tensor_tensor(((b_qs0, b_qs1)[h % 2])[:, :],
                                                b_qg[:, h * 512:(h + 1) * 512], b_rbps[:, :],
                                                ALU.mult).then_inc(sV, 1); cV.inc()
                                v.wait_ge(sP, hP + 2 * nb + 1)
                                v.reciprocal(b_rcp[:, :], ((b_sm0, b_sm1)[h % 2])[0:1, :]).then_inc(sV, 1); cV.inc()
                                v.wait_ge(sA, hA + nb + 1)
                                v.tensor_tensor(b_ao[:, h * 512:(h + 1) * 512],
                                                ((b_o0, b_o1)[h % 2])[:, :], b_rs[:, :],
                                                ALU.mult).then_inc(sV, 1); cV.inc()
                        @blk.gpsimd
                        def _(g):
                            g.wait_ge(sV, bV + 3 * H)
                            srca = b_ao[:, :].rearrange("p (hc t) -> p hc t", hc=H)
                            dsta = aoscr.ap()[s_i][:, qcol].rearrange("(hc p) t -> p hc t", p=CH)
                            g.dma_start(dsta, srca).then_inc(sG, 16); cG.inc()
                            g.wait_ge(sG, 16 * cG.v)

    if upto < 5:
        _emit_dram_dbg(nc, dbg, s_i, qscr, aoscr, sscr, sG, cG)
        return
    with ExitStack() as _st:
        c_ao = _st.enter_context(nc.sbuf_tensor(f"c{s_i}_ao", [CH, NCC * T], BF16))
        c_w0 = _st.enter_context(nc.sbuf_tensor(f"c{s_i}_w0", [CH, NCC * CH], BF16))
        c_w1 = _st.enter_context(nc.sbuf_tensor(f"c{s_i}_w1", [CH, NCC * CH], BF16))
        c_x0 = _st.enter_context(nc.sbuf_tensor(f"c{s_i}_x0", [CH, T], BF16))
        c_x1 = _st.enter_context(nc.sbuf_tensor(f"c{s_i}_x1", [CH, T], BF16))
        c_s0 = _st.enter_context(nc.sbuf_tensor(f"c{s_i}_s0", [CH, T], BF16))
        c_s1 = _st.enter_context(nc.sbuf_tensor(f"c{s_i}_s1", [CH, T], BF16))
        c_y0 = _st.enter_context(nc.psum_tensor(f"c{s_i}_y0", [CH, T], F32))
        c_y1 = _st.enter_context(nc.psum_tensor(f"c{s_i}_y1", [CH, T], F32))
        with nc.Block() as blk:
            @blk.sync
            def _(s):
                srca = aoscr.ap()[s_i].rearrange("(oc p) t -> p oc t", p=CH)
                dsta = c_ao[:, :].rearrange("p (oc t) -> p oc t", oc=NCC)
                s.dma_start(dsta, srca).then_inc(sS, 16); cS.inc()
                s.wait_ge(sS, 16 * cS.v)
        with nc.Block() as blk:
            bS, bG, bP, bV = cS.v, cG.v, cP.v, cV.v
            @blk.sync
            def _(s):
                for co in range(NCC):
                    wbuf = (c_w0, c_w1)[co % 2]
                    xbuf = (c_x0, c_x1)[co % 2]
                    if co >= 2:
                        s.wait_ge(sP, bP + co - 1)
                        s.wait_ge(sV, bV + co - 1)
                    srcw = woT.ap()[:, co * CH:(co + 1) * CH].rearrange("(oc p) c -> p oc c", p=CH)
                    dstw = wbuf[:, :].rearrange("p (oc c) -> p oc c", oc=NCC)
                    s.dma_start(dstw, srcw).then_inc(sS, 16); cS.inc()
                    s.dma_start(xbuf[:, :], xbfT.ap()[s_i][co * CH:(co + 1) * CH, :]).then_inc(sS, 16); cS.inc()
            @blk.tensor
            def _(t):
                for co in range(NCC):
                    wbuf = (c_w0, c_w1)[co % 2]
                    ps = (c_y0, c_y1)[co % 2]
                    t.wait_ge(sS, 16 * (bS + 2 * co + 1))
                    if co >= 2:
                        t.wait_ge(sV, bV + co - 1)
                    for oi in range(NCC):
                        for n in range(4):
                            ins = t.matmul(ps[:, n * 512:(n + 1) * 512],
                                           wbuf[:, oi * CH:(oi + 1) * CH],
                                           c_ao[:, oi * T + n * 512: oi * T + (n + 1) * 512],
                                           start=(oi == 0), stop=(oi == NCC - 1))
                    ins.then_inc(sP, 1); cP.inc()
            @blk.vector
            def _(v):
                for co in range(NCC):
                    xbuf = (c_x0, c_x1)[co % 2]
                    sbuf = (c_s0, c_s1)[co % 2]
                    ps = (c_y0, c_y1)[co % 2]
                    v.wait_ge(sP, bP + co + 1)
                    v.wait_ge(sS, 16 * (bS + 2 * co + 2))
                    if co >= 2:
                        v.wait_ge(sG, 16 * (bG + co - 1))
                    for nn in range(4):
                        c5 = slice(nn * 512, (nn + 1) * 512)
                        ins = v.tensor_tensor(sbuf[:, c5], ps[:, c5], xbuf[:, c5], ALU.add)
                    ins.then_inc(sV, 1); cV.inc()
            @blk.gpsimd
            def _(g):
                for co in range(NCC):
                    sbuf = (c_s0, c_s1)[co % 2]
                    g.wait_ge(sV, bV + co + 1)
                    g.dma_start(sscr.ap()[s_i][co * CH:(co + 1) * CH, :], sbuf[:, :]).then_inc(sG, 16); cG.inc()
                g.wait_ge(sG, 16 * cG.v)

    if upto < 6:
        _emit_dram_dbg(nc, dbg, s_i, qscr, aoscr, sscr, sG, cG)
        return
    with (
        nc.sbuf_tensor(f"c2{s_i}_s0", [CH, T], BF16) as c2_s0,
        nc.sbuf_tensor(f"c2{s_i}_s1", [CH, T], BF16) as c2_s1,
        nc.sbuf_tensor(f"c2{s_i}_sq", [CH, T], BF16) as c2_sq,
        nc.sbuf_tensor(f"c2{s_i}_ra", [1, T], F32) as c2_ra,
        nc.sbuf_tensor(f"c2{s_i}_rb", [1, T], F32) as c2_rb,
        nc.sbuf_tensor(f"c2{s_i}_rc", [1, T], F32) as c2_rc,
    ):
        with ExitStack() as _st:
            c2_smps = _st.enter_context(nc.psum_tensor(f"c2{s_i}_sm", [1, T], F32))
            c2_sqps = _st.enter_context(nc.psum_tensor(f"c2{s_i}_sq_ps", [1, T], F32))
            with nc.Block() as blk:
                bS, bP, bA = cS.v, cP.v, cA.v
                @blk.sync
                def _(s):
                    for co in range(NCC):
                        buf = (c2_s0, c2_s1)[co % 2]
                        if co >= 2:
                            s.wait_ge(sP, bP + co - 1)
                        s.dma_start(buf[:, :], sscr.ap()[s_i][co * CH:(co + 1) * CH, :]).then_inc(sS, 16); cS.inc()
                @blk.scalar
                def _(sc):
                    for co in range(NCC):
                        buf = (c2_s0, c2_s1)[co % 2]
                        sc.wait_ge(sS, 16 * (bS + co + 1))
                        if co >= 1:
                            sc.wait_ge(sP, bP + co)
                        sc.activation(c2_sq[:, :], buf[:, :], AF.Square).then_inc(sA, 1); cA.inc()
                @blk.tensor
                def _(t):
                    for co in range(NCC):
                        buf = (c2_s0, c2_s1)[co % 2]
                        t.wait_ge(sA, bA + co + 1)
                        for n in range(4):
                            t.matmul(c2_smps[:, n * 512:(n + 1) * 512], g_ones_bf[:, :],
                                     buf[:, n * 512:(n + 1) * 512],
                                     start=(co == 0), stop=(co == NCC - 1))
                        for n in range(4):
                            ins = t.matmul(c2_sqps[:, n * 512:(n + 1) * 512], g_ones_bf[:, :],
                                           c2_sq[:, n * 512:(n + 1) * 512],
                                           start=(co == 0), stop=(co == NCC - 1))
                        ins.then_inc(sP, 1); cP.inc()
            with nc.Block() as blk:
                bA, bV = cA.v, cV.v
                @blk.vector
                def _(v):
                    for nn in range(4):
                        c5 = slice(nn * 512, (nn + 1) * 512)
                        ins = v.tensor_scalar_mul(c2_ra[0:1, c5], c2_smps[0:1, c5], 1.0 / C)
                    ins.then_inc(sV, 1); cV.inc()
                    v.tensor_tensor(c2_rc[:, :], c2_ra[:, :], c2_ra[:, :], ALU.mult)
                    v.tensor_scalar_add(c2_rc[:, :], c2_rc[:, :], -LN_EPS).then_inc(sV, 1); cV.inc()
                    for nn in range(4):
                        c5 = slice(nn * 512, (nn + 1) * 512)
                        ins = v.scalar_tensor_tensor(c2_rb[0:1, c5], c2_sqps[0:1, c5], 1.0 / C, c2_rc[0:1, c5],
                                                     ALU.mult, ALU.subtract)
                    ins.then_inc(sV, 1); cV.inc()
                    v.wait_ge(sA, bA + 1)
                    v.reciprocal(c2_rb[:, :], c2_rb[:, :]).then_inc(sV, 1); cV.inc()
                @blk.scalar
                def _(sc):
                    sc.wait_ge(sV, bV + 3)
                    sc.activation(c2_rb[:, :], c2_rb[:, :], AF.Sqrt).then_inc(sA, 1); cA.inc()
        with ExitStack() as _st:
            c2_mbps = _st.enter_context(nc.psum_tensor(f"c2{s_i}_mb", [CH, T], F32))
            c2_rbps = _st.enter_context(nc.psum_tensor(f"c2{s_i}_rbp", [CH, T], F32))
            with nc.Block() as blk:
                bP = cP.v
                @blk.tensor
                def _(t):
                    for n in range(4):
                        t.matmul(c2_mbps[:, n * 512:(n + 1) * 512], g_onesrow[:, :],
                                 c2_ra[:, n * 512:(n + 1) * 512], start=True, stop=True)
                        ins = t.matmul(c2_rbps[:, n * 512:(n + 1) * 512], g_onesrow[:, :],
                                       c2_rb[:, n * 512:(n + 1) * 512], start=True, stop=True)
                    ins.then_inc(sP, 1); cP.inc()
                @blk.scalar
                def _(sc):
                    sc.wait_ge(sP, cP.v)
                    sc.activation(g_mb[:, :], c2_mbps[:, :], AF.Copy).then_inc(sA, 1); cA.inc()
                @blk.vector
                def _(v):
                    v.wait_ge(sP, cP.v)
                    for nn in range(4):
                        c5 = slice(nn * 512, (nn + 1) * 512)
                        ins = v.tensor_copy(g_rb[:, c5], c2_rbps[:, c5])
                    ins.then_inc(sV, 1); cV.inc()

    if dbg and ("dbg_mb" in dbg or "dbg_rbt" in dbg) and s_i == (1 if "dbg_seq1" in _DEBUG else 0):
        with nc.Block() as blk:
            @blk.gpsimd
            def _(g):
                if "dbg_mb" in dbg:
                    g.dma_start(dbg["dbg_mb"].ap(), g_mb[:, :]).then_inc(sG, 16); cG.inc()
                if "dbg_rbt" in dbg:
                    g.dma_start(dbg["dbg_rbt"].ap(), g_rb[:, :]).then_inc(sG, 16); cG.inc()
                g.wait_ge(sG, 16 * cG.v)
    if upto < 7:
        _emit_dram_dbg(nc, dbg, s_i, qscr, aoscr, sscr, sG, cG)
        return
    with ExitStack() as _st:
        d_x1 = _st.enter_context(nc.sbuf_tensor(f"d{s_i}_x1", [CH, NCC * 1024], BF16))
        d_m = _st.enter_context(nc.sbuf_tensor(f"d{s_i}_m", [CH, NHC * 1024], BF16))
        d_s2 = _st.enter_context(nc.sbuf_tensor(f"d{s_i}_s2", [CH, NCC * 1024], BF16))
        for hh in range(2):
            tcol = slice(hh * 1024, (hh + 1) * 1024)
            with nc.Block() as blk:
                bS = cS.v
                @blk.sync
                def _(s):
                    srcs = sscr.ap()[s_i][:, tcol].rearrange("(cc p) t -> p cc t", p=CH)
                    dsts = d_x1[:, :].rearrange("p (cc t) -> p cc t", cc=NCC)
                    s.dma_start(dsts, srcs).then_inc(sS, 16); cS.inc()
                @blk.vector
                def _(v):
                    v.wait_ge(sS, 16 * (bS + 1))
                    for co in range(NCC):
                        sl = slice(co * 1024, (co + 1) * 1024)
                        v.tensor_tensor(d_x1[:, sl], d_x1[:, sl], g_mb[:, tcol], ALU.subtract)
                        ins = v.tensor_tensor(d_x1[:, sl], d_x1[:, sl], g_rb[:, tcol], ALU.mult)
                    ins.then_inc(sV, 1); cV.inc()

            with ExitStack() as _st:
                d_wg0 = _st.enter_context(nc.sbuf_tensor(f"d1{s_i}{hh}_wg0", [CH, NCC * CH], BF16))
                d_wg1 = _st.enter_context(nc.sbuf_tensor(f"d1{s_i}{hh}_wg1", [CH, NCC * CH], BF16))
                d_wu0 = _st.enter_context(nc.sbuf_tensor(f"d1{s_i}{hh}_wu0", [CH, NCC * CH], BF16))
                d_wu1 = _st.enter_context(nc.sbuf_tensor(f"d1{s_i}{hh}_wu1", [CH, NCC * CH], BF16))
                d_sg0 = _st.enter_context(nc.sbuf_tensor(f"d1{s_i}{hh}_sg0", [CH, 1024], BF16))
                d_sg1 = _st.enter_context(nc.sbuf_tensor(f"d1{s_i}{hh}_sg1", [CH, 1024], BF16))
                d_g0 = _st.enter_context(nc.psum_tensor(f"d1{s_i}{hh}_g0", [CH, 1024], F32))
                d_g1 = _st.enter_context(nc.psum_tensor(f"d1{s_i}{hh}_g1", [CH, 1024], F32))
                d_u0 = _st.enter_context(nc.psum_tensor(f"d1{s_i}{hh}_u0", [CH, 1024], F32))
                d_u1 = _st.enter_context(nc.psum_tensor(f"d1{s_i}{hh}_u1", [CH, 1024], F32))
                with nc.Block() as blk:
                    bS, bP, bA, bV = cS.v, cP.v, cA.v, cV.v
                    @blk.sync
                    def _(s):
                        for hc in range(NHC):
                            wg = (d_wg0, d_wg1)[hc % 2]
                            wu = (d_wu0, d_wu1)[hc % 2]
                            if hc >= 2:
                                s.wait_ge(sP, bP + 2 * hc - 2)
                            srcg = wgT.ap()[:, hc * CH:(hc + 1) * CH].rearrange("(cc p) h -> p cc h", p=CH)
                            s.dma_start(wg[:, :].rearrange("p (cc h) -> p cc h", cc=NCC), srcg).then_inc(sS, 16); cS.inc()
                            srcu = wuT.ap()[:, hc * CH:(hc + 1) * CH].rearrange("(cc p) h -> p cc h", p=CH)
                            s.dma_start(wu[:, :].rearrange("p (cc h) -> p cc h", cc=NCC), srcu).then_inc(sS, 16); cS.inc()
                    @blk.tensor
                    def _(t):
                        for hc in range(NHC):
                            wg = (d_wg0, d_wg1)[hc % 2]
                            wu = (d_wu0, d_wu1)[hc % 2]
                            gps = (d_g0, d_g1)[hc % 2]
                            ups = (d_u0, d_u1)[hc % 2]
                            t.wait_ge(sS, 16 * (bS + 2 * hc + 2))
                            if hc >= 2:
                                t.wait_ge(sA, bA + hc - 1)
                            for ci in range(NCC):
                                for n in range(2):
                                    ins = t.matmul(gps[:, n * 512:(n + 1) * 512],
                                                   wg[:, ci * CH:(ci + 1) * CH],
                                                   d_x1[:, ci * 1024 + n * 512: ci * 1024 + (n + 1) * 512],
                                                   start=(ci == 0), stop=(ci == NCC - 1))
                            ins.then_inc(sP, 1); cP.inc()
                            if hc >= 2:
                                t.wait_ge(sV, bV + hc - 1)
                            for ci in range(NCC):
                                for n in range(2):
                                    ins = t.matmul(ups[:, n * 512:(n + 1) * 512],
                                                   wu[:, ci * CH:(ci + 1) * CH],
                                                   d_x1[:, ci * 1024 + n * 512: ci * 1024 + (n + 1) * 512],
                                                   start=(ci == 0), stop=(ci == NCC - 1))
                            ins.then_inc(sP, 1); cP.inc()
                    @blk.scalar
                    def _(sc):
                        for hc in range(NHC):
                            gps = (d_g0, d_g1)[hc % 2]
                            sg = (d_sg0, d_sg1)[hc % 2]
                            sc.wait_ge(sP, bP + 2 * hc + 1)
                            if hc >= 2:
                                sc.wait_ge(sV, bV + hc - 1)
                            sc.activation(sg[:, :], gps[:, :], AF.Silu).then_inc(sA, 1); cA.inc()
                    @blk.vector
                    def _(v):
                        for hc in range(NHC):
                            ups = (d_u0, d_u1)[hc % 2]
                            sg = (d_sg0, d_sg1)[hc % 2]
                            v.wait_ge(sA, bA + hc + 1)
                            v.wait_ge(sP, bP + 2 * hc + 2)
                            for nn in range(2):
                                c5 = slice(nn * 512, (nn + 1) * 512)
                                ins = v.tensor_tensor(d_m[:, hc * 1024 + nn * 512: hc * 1024 + (nn + 1) * 512],
                                                      sg[:, c5], ups[:, c5], ALU.mult)
                            ins.then_inc(sV, 1); cV.inc()

            if dbg and hh == 0 and s_i == (1 if "dbg_seq1" in _DEBUG else 0) and ("dbg_x1" in dbg or "dbg_m" in dbg):
                with nc.Block() as blk:
                    @blk.gpsimd
                    def _(g):
                        if "dbg_x1" in dbg:
                            src = d_x1[:, :].rearrange("p (cc t) -> p cc t", cc=NCC)
                            dst = dbg["dbg_x1"].ap()[:, 0:1024].rearrange("(cc p) t -> p cc t", p=CH)
                            g.dma_start(dst, src).then_inc(sG, 16); cG.inc()
                        if "dbg_m" in dbg:
                            src = d_m[:, :].rearrange("p (hc t) -> p hc t", hc=NHC)
                            dst = dbg["dbg_m"].ap()[:, 0:1024].rearrange("(hc p) t -> p hc t", p=CH)
                            g.dma_start(dst, src).then_inc(sG, 16); cG.inc()
                        g.wait_ge(sG, 16 * cG.v)
            with ExitStack() as _st:
                d2_w0 = _st.enter_context(nc.sbuf_tensor(f"d2{s_i}{hh}_w0", [CH, NHC * CH], BF16))
                d2_w1 = _st.enter_context(nc.sbuf_tensor(f"d2{s_i}{hh}_w1", [CH, NHC * CH], BF16))
                d2_p0 = _st.enter_context(nc.psum_tensor(f"d2{s_i}{hh}_p0", [CH, 512], F32))
                d2_p1 = _st.enter_context(nc.psum_tensor(f"d2{s_i}{hh}_p1", [CH, 512], F32))
                d2_p2 = _st.enter_context(nc.psum_tensor(f"d2{s_i}{hh}_p2", [CH, 512], F32))
                d2_p3 = _st.enter_context(nc.psum_tensor(f"d2{s_i}{hh}_p3", [CH, 512], F32))
                dps = (d2_p0, d2_p1, d2_p2, d2_p3)
                with nc.Block() as blk:
                    bS, bP, bV = cS.v, cP.v, cV.v
                    @blk.sync
                    def _(s):
                        for co in range(NCC):
                            wbuf = (d2_w0, d2_w1)[co % 2]
                            if co >= 2:
                                s.wait_ge(sP, bP + 2 * co - 2)
                            srcw = wdT.ap()[:, co * CH:(co + 1) * CH].rearrange("(hc p) c -> p hc c", p=CH)
                            s.dma_start(wbuf[:, :].rearrange("p (hc c) -> p hc c", hc=NHC), srcw).then_inc(sS, 16); cS.inc()
                    @blk.tensor
                    def _(t):
                        for co in range(NCC):
                            wbuf = (d2_w0, d2_w1)[co % 2]
                            t.wait_ge(sS, 16 * (bS + co + 1))
                            for n in range(2):
                                idx = (2 * co + n) % 4
                                if "d2serial" in _DEBUG and 2 * co + n >= 1:
                                    t.wait_ge(sV, bV + 2 * co + n)
                                if 2 * co + n >= 4:
                                    t.wait_ge(sV, bV + 2 * co + n - 3)
                                for hc in range(NHC):
                                    ins = t.matmul(dps[idx][:, :],
                                                   wbuf[:, hc * CH:(hc + 1) * CH],
                                                   d_m[:, hc * 1024 + n * 512: hc * 1024 + (n + 1) * 512],
                                                   start=(hc == 0), stop=(hc == NHC - 1))
                                ins.then_inc(sP, 1); cP.inc()
                    @blk.vector
                    def _(v):
                        for co in range(NCC):
                            for n in range(2):
                                idx = (2 * co + n) % 4
                                v.wait_ge(sP, bP + 2 * co + n + 1)
                                v.tensor_tensor(d_s2[:, co * 1024 + n * 512: co * 1024 + (n + 1) * 512],
                                                dps[idx][:, :],
                                                d_x1[:, co * 1024 + n * 512: co * 1024 + (n + 1) * 512],
                                                ALU.add).then_inc(sV, 1); cV.inc()
                    @blk.gpsimd
                    def _(g):
                        for co in range(NCC):
                            g.wait_ge(sV, bV + 2 * co + 2)
                            g.dma_start(s2scr.ap()[s_i][co * CH:(co + 1) * CH, hh * 1024:(hh + 1) * 1024],
                                        d_s2[:, co * 1024:(co + 1) * 1024]).then_inc(sG, 16); cG.inc()
                        g.wait_ge(sG, 16 * cG.v)

            with ExitStack() as _st:
                d3_stm0 = _st.enter_context(nc.sbuf_tensor(f"d3{s_i}{hh}_stm0", [CH, C], BF16))
                d3_stm1 = _st.enter_context(nc.sbuf_tensor(f"d3{s_i}{hh}_stm1", [CH, C], BF16))
                d3_o0 = _st.enter_context(nc.sbuf_tensor(f"d3{s_i}{hh}_o0", [CH, C], F32))
                d3_o1 = _st.enter_context(nc.sbuf_tensor(f"d3{s_i}{hh}_o1", [CH, C], F32))
                d3_junk = _st.enter_context(nc.sbuf_tensor(f"d3{s_i}{hh}_junk", [CH, C], BF16))
                d3_cols = _st.enter_context(nc.sbuf_tensor(f"d3{s_i}{hh}_cols", [CH, 8], F32))
                with nc.Block() as blk:
                    bA, bV, bS = cA.v, cV.v, cS.v
                    ndv = 9
                    nac = 1
                    @blk.sync
                    def _(s):
                        for tt in range(8):
                            stm = (d3_stm0, d3_stm1)[tt % 2]
                            obuf = (d3_o0, d3_o1)[tt % 2]
                            if tt >= 2:
                                s.wait_ge(sV, bV + (tt - 2) * ndv + ndv)
                            tcolg = hh * 1024 + tt * CH
                            s.dma_start_transpose(stm[:, :],
                                                  s2scr.ap()[s_i][:, tcolg:tcolg + CH]).then_inc(sS, 16); cS.inc()
                            s.wait_ge(sV, bV + tt * ndv + ndv)
                            row0 = hh * 1024 + tt * CH
                            s.dma_start(out.ap()[s_i][row0:row0 + CH, :], obuf[:, :]).then_inc(sS, 16); cS.inc()
                        s.wait_ge(sS, 16 * cS.v)
                    @blk.vector
                    def _(v):
                        for tt in range(8):
                            stm = (d3_stm0, d3_stm1)[tt % 2]
                            obuf = (d3_o0, d3_o1)[tt % 2]
                            ttV = bV + tt * ndv
                            ttA = bA + tt * nac
                            def vstep(ins):
                                ins.then_inc(sV, 1); cV.inc()
                                v.wait_ge(sV, cV.v)
                            v.wait_ge(sS, 16 * (bS + 2 * tt + 1))
                            vstep(v.reduce_sum(d3_cols[:, 0:1], stm[:, :], mybir.AxisListType.X))
                            v.tensor_tensor(d3_junk[:, :], stm[:, :], stm[:, :], ALU.mult)
                            vstep(v.reduce_sum(d3_cols[:, 1:2], d3_junk[:, :], mybir.AxisListType.X))
                            vstep(v.tensor_scalar_mul(d3_cols[:, 2:3], d3_cols[:, 0:1], 1.0 / C))
                            vstep(v.tensor_tensor(d3_cols[:, 3:4], d3_cols[:, 2:3], d3_cols[:, 2:3], ALU.mult))
                            vstep(v.scalar_tensor_tensor(d3_cols[:, 4:5], d3_cols[:, 1:2], 1.0 / C,
                                                         d3_cols[:, 3:4], ALU.mult, ALU.subtract))
                            vstep(v.tensor_scalar_add(d3_cols[:, 4:5], d3_cols[:, 4:5], LN_EPS))
                            v.wait_ge(sA, ttA + 1)
                            vstep(v.reciprocal(d3_cols[:, 5:6], d3_cols[:, 4:5]))
                            vstep(v.scalar_tensor_tensor(d3_cols[:, 6:7], d3_cols[:, 2:3], -1.0,
                                                         d3_cols[:, 5:6], ALU.mult, ALU.mult))
                            if tt >= 2:
                                v.wait_ge(sS, 16 * (bS + 2 * (tt - 2) + 2))
                            v.tensor_scalar(obuf[:, :], stm[:, :], d3_cols[:, 5:6], d3_cols[:, 6:7],
                                            ALU.mult, ALU.add).then_inc(sV, 1); cV.inc()
                    @blk.scalar
                    def _(sc):
                        for tt in range(8):
                            ttV = bV + tt * ndv
                            sc.wait_ge(sV, ttV + 6)
                            sc.activation(d3_cols[:, 4:5], d3_cols[:, 4:5], AF.Sqrt).then_inc(sA, 1); cA.inc()

    if dbg and "dbg_s2" in dbg and s_i == (1 if "dbg_seq1" in _DEBUG else 0):
        with nc.Block() as blk:
            @blk.gpsimd
            def _(g):
                g.dma_start(dbg["dbg_s2"].ap(), s2scr.ap()[s_i]).then_inc(sG, 16); cG.inc()
                g.wait_ge(sG, 16 * cG.v)
    _emit_dram_dbg(nc, dbg, s_i, qscr, aoscr, sscr, sG, cG)


def _emit_dram_dbg(nc, dbg, s_i, qscr, aoscr, sscr, sG, cG):
    want_seq = 1 if "dbg_seq1" in _DEBUG else 0
    if dbg and s_i == want_seq:
        with nc.Block() as blk:
            @blk.gpsimd
            def _(g):
                pairs = [("dbg_q", qscr), ("dbg_ao", aoscr), ("dbg_s", sscr)]
                for name, src in pairs:
                    if name in dbg:
                        g.dma_start(dbg[name].ap(), src.ap()[s_i]).then_inc(sG, 16); cG.inc()
                g.wait_ge(sG, 16 * cG.v)


def kernel(L0_attn_x, L0_attn_W_qkv, L0_attn_W_o, L0_ffn_wq_gate, L0_ffn_s_gate,
           L0_ffn_wq_up, L0_ffn_s_up, L0_ffn_wq_down, L0_ffn_s_down):
    consts = build_host_consts()
    nc, dbg = build_program()

    x = np.ascontiguousarray(np.asarray(L0_attn_x, dtype=np.float32))
    base = dict(
        w_qkv=np.ascontiguousarray(np.asarray(L0_attn_W_qkv, dtype=np.float32)),
        w_o=np.ascontiguousarray(np.asarray(L0_attn_W_o, dtype=np.float32)),
        wq_gate=np.ascontiguousarray(np.asarray(L0_ffn_wq_gate, dtype=np.int32)),
        wq_up=np.ascontiguousarray(np.asarray(L0_ffn_wq_up, dtype=np.int32)),
        wq_down=np.ascontiguousarray(np.asarray(L0_ffn_wq_down, dtype=np.int32)),
        s_gate=np.ascontiguousarray(np.asarray(L0_ffn_s_gate, dtype=np.float32)),
        s_up=np.ascontiguousarray(np.asarray(L0_ffn_s_up, dtype=np.float32)),
        s_down=np.ascontiguousarray(np.asarray(L0_ffn_s_down, dtype=np.float32)),
        **consts,
    )
    in_maps = []
    for c in range(NCORE):
        m = dict(base)
        m["x"] = np.ascontiguousarray(x[2 * c:2 * c + 2])
        in_maps.append(m)
    res = run_bass_kernel_spmd(nc, in_maps, core_ids=list(range(NCORE)))
    full = np.empty((B, T, C), dtype=np.float32)
    for c in range(NCORE):
        full[2 * c:2 * c + 2] = np.asarray(res.results[c]["out"])
    return full
